# version 12
# baseline (speedup 1.0000x reference)
import numpy as np
import ml_dtypes
from contextlib import ExitStack

import concourse.bass as bass
import concourse.mybir as mybir
from concourse.bass_utils import run_bass_kernel_spmd
import concourse.bass_utils as _bu

if not getattr(_bu, "_ldw_opt_patched", False):
    _orig_gwa = _bu.get_walrus_args
    def _gwa_ldw(*a, **k):
        return [x.replace("--enable-ldw-opt=false", "--enable-ldw-opt=true")
                for x in _orig_gwa(*a, **k)]
    _bu.get_walrus_args = _gwa_ldw
    _bu._ldw_opt_patched = True

F32 = mybir.dt.float32
BF16 = mybir.dt.bfloat16
AF = mybir.ActivationFunctionType
ALU = mybir.AluOpType
AX = mybir.AxisListType
f32 = np.float32

D = 1024
H = 16
HD = 64
NKC = 8
TT = 256
NTC = 2
EPS_LN = f32(1e-5)
EPS_DEN = f32(1e-6)
EPS_Q = f32(1e-8)
QMAX = 127.0


class Sched:

    def __init__(self):
        self.ops = []
        self.counts = {}
        self.res_w = {}
        self.res_r = {}
        self.excl = {}
        self.sem_step = {}
        self.dma_sems = []
        self.last_transpose = None
        self.watermark = {}

    def sem(self, name, step):
        if name not in self.counts:
            self.counts[name] = 0
            self.sem_step[name] = step
            if step == 16:
                self.dma_sems.append(name)

    def add(self, engine, emit, reads=(), writes=(), sem=None, step=1,
            is_transpose=False, is_dma=False, name="", extra_waits=()):
        if sem is None:
            sem = engine
        self.sem(sem, step)
        need = {}

        def req(dep):
            if dep is None:
                return
            s, c = dep
            if self.sem_step.get(s) == 16:
                c = self.counts[s]
            if c <= self.watermark.get((engine, s), 0):
                return
            if need.get(s, 0) < c:
                need[s] = c

        for r in reads:
            req(self.excl.get(r) if r.startswith("ps!") else self.res_w.get(r))
        for w in writes:
            if w.startswith("ps!"):
                req(self.excl.get(w))
            else:
                req(self.res_w.get(w))
                for s, c in self.res_r.get(w, {}).items():
                    req((s, c))
        if is_transpose:
            for s in self.dma_sems:
                if self.counts[s]:
                    req((s, self.counts[s]))
        if is_dma and not is_transpose and self.last_transpose is not None:
            req(self.last_transpose)
        for d in extra_waits:
            req(d)

        self.counts[sem] += step
        cnt = self.counts[sem]
        for s, c in need.items():
            self.watermark[(engine, s)] = max(self.watermark.get((engine, s), 0), c)
        self.ops.append(dict(engine=engine, emit=emit, waits=sorted(need.items()),
                             sem=sem, step=step, name=name))
        tag = (sem, cnt)
        for r in reads:
            if r.startswith("ps!"):
                self.excl[r] = tag
            else:
                self.res_r.setdefault(r, {})[sem] = cnt
        for w in writes:
            if w.startswith("ps!"):
                self.excl[w] = tag
            else:
                self.res_w[w] = tag
                self.res_r[w] = {}
        if is_transpose:
            self.last_transpose = tag
        return tag


def build_program(S=4096, phases=99, asub=9):
    NTILE = S // TT
    nc = bass.Bass("TRN2", target_bir_lowering=False, debug=False)

    x_h = nc.dram_tensor("x", [S, D], F32, kind="ExternalInput").ap()
    out_h = nc.dram_tensor("out", [S, D], F32, kind="ExternalOutput").ap()
    w_h = {}
    b_h = {}
    for n in ("q", "k", "v", "o", "f"):
        w_h[n] = nc.dram_tensor(f"w{n}", [D, D], F32, kind="ExternalInput").ap()
        b_h[n] = nc.dram_tensor(f"b{n}", [D], F32, kind="ExternalInput").ap()
    mask_h = nc.dram_tensor("maskc", [128, 512], BF16, kind="ExternalInput").ap()
    id_h = nc.dram_tensor("identc", [128, 128], F32, kind="ExternalInput").ap()

    es = ExitStack()
    sbuf = lambda n, shp, dt: es.enter_context(nc.sbuf_tensor(n, shp, dt))
    psum = lambda n: es.enter_context(nc.psum_tensor(n, [128, 512], F32))

    XRES = [sbuf(f"xres{p}", [128, NTC, D], F32) for p in range(2)]
    X2 = sbuf("x2", [128, NTC, D], F32)
    WBUF = sbuf("wbuf", [128, NTC * D], F32)
    XRT = sbuf("xrt", [128, NTC * D], BF16)
    XRFM = sbuf("xrfm", [128, NTC, NKC, 128], BF16)
    QTOK = sbuf("qtok", [128, NTC * D], BF16)
    KTOK = sbuf("ktok", [128, NTC, D], BF16)
    QFM = sbuf("qfm", [128, NTC, NKC, 128], BF16)
    KFM = sbuf("kfm", [128, NTC, NKC, 128], BF16)
    VTOK = sbuf("vtok", [128, NTC, H, HD + 1], BF16)
    EQ = sbuf("eq", [128, NTC * D], BF16)
    PQ = sbuf("pq", [128, NTC * D], BF16)
    EK = sbuf("ek", [128, NTC * D], BF16)
    PK = sbuf("pk", [128, NTC * D], BF16)
    SGB = sbuf("sgb", [128, NTC * D], BF16)
    PCOPY = sbuf("pcopy", [128, NTC, D], F32)
    ATS = sbuf("ats", [128, H, 128], BF16)
    ATN = sbuf("atn", [128, NTC, D], BF16)
    AFM = sbuf("afm", [128, NTC, NKC, 128], BF16)
    XQFM = sbuf("xqfm", [128, NTC, NKC, 128], BF16)
    XQT = XRT
    DEN = sbuf("den", [128, H], F32)
    REC = sbuf("rec", [128, H], F32)
    BNB = sbuf("bnb", [128, 12], F32)
    AGG = sbuf("agg", [128, 4, NTC, 2], F32)
    DS = sbuf("ds", [128, 48], F32)
    STSB = sbuf("stsb", [128, H // 2, HD + 1], BF16)
    ROWQ = sbuf("rowq", [2, NTC, 128], BF16)
    ROWF = sbuf("rowf", [1, NTC, 128], BF16)
    RMFA = sbuf("rmfa", [1, D], BF16)
    ROWP = sbuf("rowp", [1, NTC, 128], BF16)
    RM = {n: sbuf(f"rm{n}", [2, D], BF16) for n in ("q", "k", "v", "f")}
    RMO = sbuf("rmo", [1, D], BF16)
    ONES1 = sbuf("ones1", [1, 128], BF16)
    ONESD = sbuf("onesd", [1, D], BF16)
    ONESC = sbuf("onesc", [128, 1], BF16)
    MASK = sbuf("maskt", [128, 512], BF16)
    IDENT = sbuf("identt", [128, 128], F32)
    WF_ = {n: sbuf(f"w{n}f", [128, NKC, D], BF16) for n in ("q", "k", "v", "o")}
    WDQF = sbuf("wdqf", [128, NKC, D], BF16)
    WDQN = sbuf("wdqn", [128, D], BF16)
    WSC = sbuf("wsc", [128, 6], F32)
    WNAT = PCOPY[:, 0, :]

    PS = [psum(f"ps{i}") for i in range(6)]
    SPS = [psum(f"sps{i}") for i in range(2)]

    sc = Sched()

    DSC = dict(r1=0, nmr=2, tmp=4, r2=6, nm2=8, r3=10, nm3=12, amx=14,
               xs=16, sinv=18, nsinv=20, rs=22, s2=24, r4=26, sfin=28,
               nmp=30, xs2=32, vf=34, tmp2=36, nrs=38, s2n=40, tsc2=42, tscf=44)

    def ds(k, tc=None):
        c = DSC[k]
        return DS[:, c:c + 2] if tc is None else DS[:, c + tc:c + tc + 1]

    def dma(stream, dst, src, reads, writes, transpose=False, name=""):
        s = f"Q.{stream}"
        sc.sem(s, 16)
        return sc.add("Q", lambda: nc.sync.dma_start(dst, src, transpose=transpose),
                      reads=reads, writes=writes, sem=s, step=16,
                      is_transpose=transpose, is_dma=True, name=name)

    def gdma(dst, src, reads, writes, name=""):
        sc.sem("Gd", 16)
        ew = [("Gd", sc.counts["Gd"])] if sc.counts["Gd"] else []
        return sc.add("G", lambda: nc.gpsimd.dma_start(dst, src),
                      reads=reads, writes=writes, sem="Gd", step=16,
                      is_dma=True, name=name, extra_waits=ew)

    op = sc.add

    dma("ld", MASK[:], mask_h[:], [], ["MASK"], name="mask")
    dma("ld", IDENT[:], id_h[:], [], ["IDENT"], name="ident")
    op("V", lambda: nc.vector.memset(ONES1[:], 1.0), [], ["ONES1"], name="ms1")
    op("V", lambda: nc.vector.memset(ONESD[:], 1.0), [], ["ONESD"], name="msD")
    op("V", lambda: nc.vector.memset(ONESC[:], 1.0), [], ["ONESC"], name="msC")
    op("V", lambda: nc.vector.memset(ROWQ[0:2], 1.0), [], ["ROWQ1"], name="msRQ")
    op("V", lambda: nc.vector.memset(VTOK[:, :, :, HD:HD + 1], 1.0), [], ["VT1"],
       name="msVT")

    for n in ("q", "k", "v", "o"):
        gdma(WF_[n][:], w_h[n].rearrange("(a b) d -> b a d", a=NKC),
             [], [f"W{n}"], name=f"w{n}cast")
    for n in ("q", "k", "v"):
        gdma(RM[n][1:2, :], b_h[n].rearrange("(a d) -> a d", a=1), [], [f"RM{n}1"],
             name=f"b{n}")
    gdma(RMO[0:1, :], b_h["o"].rearrange("(a d) -> a d", a=1), [], ["RMO0"], name="bo")
    gdma(RM["f"][0:1, :], b_h["f"].rearrange("(a d) -> a d", a=1), [], ["RMf0"],
         name="bfn")

    for n in ("q", "k", "v"):
        for hf in range(2):
            def em_w1(n=n, hf=hf):
                r = None
                for kc in range(NKC):
                    r = nc.tensor.matmul(PS[4][0:1, 0:512], ONESC[:],
                                         WF_[n][:, kc, hf * 512:(hf + 1) * 512],
                                         start=(kc == 0), stop=(kc == NKC - 1))
                return r
            op("T", em_w1, [f"W{n}", "ONESC"], ["ps!4"], name=f"w1{n}{hf}")
            op("V", (lambda n=n, hf=hf: nc.vector.tensor_copy(
                RM[n][0:1, hf * 512:(hf + 1) * 512], PS[4][0:1, 0:512])),
                ["ps!4"], [f"RM{n}0"], name=f"w1e{n}{hf}")

    def st_wquant():
        for oc in range(NKC):
            WN = WNAT[:, (oc % 2) * D:(oc % 2) * D + D] if False else WNAT[:]
            dma("wfl", WNAT, w_h["f"][oc * 128:(oc + 1) * 128, :], ["PC.0"], ["PC.0"],
                name=f"wf{oc}")
            op("V", lambda: nc.vector.tensor_reduce(WSC[:, 0:1], WNAT, axis=AX.X,
                                                    op=ALU.max, apply_absolute_value=True),
               ["PC.0"], ["WSC"], name="wam")
            op("V", lambda: nc.vector.tensor_scalar(WSC[:, 1:2], WSC[:, 0:1], float(EPS_Q),
                                                    1.0 / QMAX, op0=ALU.max,
                                                    op1=ALU.mult),
               ["WSC"], ["WSC"], name="wsc")
            op("V", lambda: nc.vector.reciprocal(WSC[:, 2:3], WSC[:, 1:2]),
               ["WSC"], ["WSC"], name="wsi")
            op("V", lambda: nc.vector.tensor_scalar(WSC[:, 3:4], WSC[:, 1:2], -1.0,
                                                    None, op0=ALU.mult),
               ["WSC"], ["WSC"], name="wns")
            op("V", lambda: nc.vector.tensor_scalar(WSC[:, 4:5], WSC[:, 3:4], 128.0,
                                                    None, op0=ALU.mult),
               ["WSC"], ["WSC"], name="wnb")
            op("S", lambda: nc.scalar.activation(WBUF[:, 0:D], WNAT, AF.Identity,
                                                 bias=f32(0.5), scale=WSC[:, 2:3]),
               ["PC.0", "WSC"], ["WB.0"], name="wu")
            for k in range(8):
                half = float(2 ** (6 - k))
                vin = WBUF[:, 0:D] if k == 0 else (WNAT[:] if k % 2 == 1 else WBUF[:, D:2 * D])
                vout = WNAT[:] if k % 2 == 0 else WBUF[:, D:2 * D]
                vin_r = "WB.0" if k == 0 else ("PC.1" if k % 2 == 1 else "WB.1")
                vout_r = "PC.1" if k % 2 == 0 else "WB.1"
                op("S", (lambda vin=vin: nc.scalar.activation(SGB[:, 0:D], vin, AF.Sign)),
                   [vin_r], ["SG.0"], name=f"wsg{k}")
                op("V", (lambda vin=vin, vout=vout, half=half:
                         nc.vector.scalar_tensor_tensor(vout, SGB[:, 0:D], -half, vin,
                                                        op0=ALU.mult, op1=ALU.add)),
                   ["SG.0", vin_r], [vout_r], name=f"wvs{k}")
            op("V", lambda: nc.vector.scalar_tensor_tensor(WNAT[:], WBUF[:, D:2 * D],
                                                           -127.5, WBUF[:, 0:D],
                                                           op0=ALU.add, op1=ALU.subtract),
               ["WB.1", "WB.0"], ["WFN"], name="wfneg")
            op("V", lambda: nc.vector.tensor_scalar(WDQN[:], WNAT[:], WSC[:, 3:4],
                                                    WSC[:, 4:5], op0=ALU.mult, op1=ALU.add),
               ["WFN", "WSC"], ["WDQN"], name="wdq")
            dma("trw", WDQF[:, :, oc * 128:(oc + 1) * 128], WDQN[:], ["WDQN"], ["WDQF"],
                transpose=True, name=f"trw{oc}")

        for hf in range(2):
            def em_w1f(hf=hf):
                r = None
                for kc in range(NKC):
                    r = nc.tensor.matmul(PS[4][0:1, 0:512], ONESC[:],
                                         WDQF[:, kc, hf * 512:(hf + 1) * 512],
                                         start=(kc == 0), stop=(kc == NKC - 1))
                return r
            op("T", em_w1f, ["WDQF", "ONESC"], ["ps!4"], name=f"w1f{hf}")
            op("V", (lambda hf=hf: nc.vector.tensor_scalar(
                RMFA[0:1, hf * 512:(hf + 1) * 512], PS[4][0:1, 0:512], 128.0,
                None, op0=ALU.mult)),
                ["ps!4"], ["RMFA"], name=f"w1fe{hf}")


    ATPERM = [0, 2, 4, 6, 8, 10, 12, 14, 1, 3, 5, 7, 9, 11, 13, 15]
    ATSLOT = {h: i for i, h in enumerate(ATPERM)}

    def stA(t):
        p = t % 2
        XR = XRES[p]
        xr_res = f"XRES{p}"
        src = x_h[t * TT:(t + 1) * TT, :].rearrange("(a b) d -> b a d", a=NTC)
        dma(f"xl{p}", XR[:], src, [f"{xr_res}.0", f"{xr_res}.1"],
            [f"{xr_res}.0", f"{xr_res}.1"], name=f"xload{t}")
        for tc in range(NTC):
            op("V", (lambda tc=tc, XR=XR: nc.vector.bn_stats(BNB[:, 0:6], XR[:, tc, 0:512])),
               [f"{xr_res}.{tc}"], ["BNB"], name=f"bs1a{t}.{tc}")
            op("V", (lambda tc=tc, XR=XR: nc.vector.bn_stats(BNB[:, 6:12], XR[:, tc, 512:D])),
               [f"{xr_res}.{tc}"], ["BNB"], name=f"bs1b{t}.{tc}")
            op("V", (lambda tc=tc: nc.vector.bn_aggr(AGG[:, 0, tc, :], BNB[:])),
               ["BNB"], ["AGG1"], name=f"ba1{t}.{tc}")
        op("S", lambda: nc.scalar.activation(ds("tmp"), AGG[:, 0, :, 1], AF.Ln,
                                             bias=EPS_LN),
           ["AGG1"], ["DS.tmp"], name=f"ln1l{t}")
        op("S", lambda: nc.scalar.activation(ds("r1"), ds("tmp"), AF.Exp,
                                             bias=f32(0.0), scale=f32(-0.5)),
           ["DS.tmp"], ["DS.r1"], name=f"ln1e{t}")
        op("V", lambda: nc.vector.scalar_tensor_tensor(ds("nmr"), AGG[:, 0, :, 0],
                                                       -1.0, ds("r1"),
                                                       op0=ALU.mult, op1=ALU.mult),
           ["AGG1", "DS.r1"], ["DS.nmr"], name=f"nmr{t}")
        for tc in range(NTC):
            op("S", (lambda tc=tc, XR=XR: nc.scalar.activation(
                XRT[:, tc * D:(tc + 1) * D], XR[:, tc, :], AF.Identity,
                bias=f32(0.0), scale=ds("r1", tc))),
                [f"{xr_res}.{tc}", "DS.r1"], ["XRT"], name=f"xr{t}.{tc}")
        dma("trxr", XRFM[:], XRT[:], ["XRT"], ["XRFM"], transpose=True,
            name=f"trxr{t}")

    def stA2(t):
        for tc in range(NTC):
            op("T", (lambda tc=tc: nc.tensor.transpose(
                PS[5][0:1, 384:512], ds("nmr", tc), IDENT[:])),
                ["DS.nmr", "IDENT"], ["ps!5"], name=f"t1{t}.{tc}")
            op("S", (lambda tc=tc: nc.scalar.copy(
                ROWQ[0:1, tc, :], PS[5][0:1, 384:512])),
                ["ps!5"], ["ROWQ0"], name=f"t1e{t}.{tc}")

    def stB(t):
        for tc in range(NTC):
            for hf in range(2):
                def em_qkv(tc=tc, hf=hf):
                    r = None
                    for kc in range(NKC):
                        st = XRFM[:, tc, kc, :]
                        for i, w in enumerate((WF_["q"], WF_["k"], WF_["v"])):
                            r = nc.tensor.matmul(
                                PS[i][:, 0:512], st,
                                w[:, kc, hf * 512:(hf + 1) * 512],
                                start=(kc == 0), stop=False)
                    for i, n in enumerate(("q", "k", "v")):
                        r = nc.tensor.matmul(
                            PS[i][:, 0:512], ROWQ[:, tc, :],
                            RM[n][:, hf * 512:(hf + 1) * 512],
                            start=False, stop=True)
                    return r
                op("T", em_qkv,
                   ["XRFM", "ROWQ0", "ROWQ1", "Wq", "Wk", "Wv", "RMq0", "RMq1",
                    "RMk0", "RMk1", "RMv0", "RMv1"],
                   ["ps!0", "ps!1", "ps!2"], name=f"qkv{t}.{tc}.{hf}")
                sl = slice((tc * 2 + hf) * 512, (tc * 2 + hf + 1) * 512)
                op("S", (lambda sl=sl: nc.scalar.activation(
                    EQ[:, sl], PS[0][:, 0:512], AF.Exp)),
                    ["ps!0"], ["EQ"], name=f"eq{t}.{tc}.{hf}")
                op("S", (lambda sl=sl: nc.scalar.activation(
                    PQ[:, sl], PS[0][:, 0:512], AF.Relu)),
                    ["ps!0"], ["PQ"], name=f"pq{t}.{tc}.{hf}")
                op("S", (lambda sl=sl: nc.scalar.activation(
                    EK[:, sl], PS[1][:, 0:512], AF.Exp)),
                    ["ps!1"], ["EK"], name=f"ek{t}.{tc}.{hf}")
                op("S", (lambda sl=sl: nc.scalar.activation(
                    PK[:, sl], PS[1][:, 0:512], AF.Relu)),
                    ["ps!1"], ["PK"], name=f"pk{t}.{tc}.{hf}")
                op("S", (lambda tc=tc, hf=hf: nc.scalar.activation(
                    VTOK[:, tc, hf * 8:(hf + 1) * 8, 0:HD], PS[2][:, 0:512],
                    AF.Identity)),
                    ["ps!2"], ["VTOK"], name=f"ve{t}.{tc}.{hf}")
        op("V", lambda: nc.vector.scalar_tensor_tensor(QTOK[:], EQ[:], 1.0,
                                                       PQ[:], op0=ALU.min,
                                                       op1=ALU.add),
           ["EQ", "PQ"], ["QTOK"], name=f"qel{t}")
        op("V", lambda: nc.vector.scalar_tensor_tensor(
            KTOK[:].rearrange("p a d -> p (a d)"), EK[:], 1.0, PK[:],
            op0=ALU.min, op1=ALU.add),
           ["EK", "PK"], ["KTOK"], name=f"kel{t}")
        dma("trq", QFM[:], QTOK[:], ["QTOK"], ["QFM"], transpose=True,
            name=f"trq{t}")
        dma("trk", KFM[:], KTOK[:].rearrange("p a d -> p (a d)"), ["KTOK"], ["KFM"],
            transpose=True, name=f"trk{t}")

    def stC(t):
        for tc in range(NTC):
            g = t * NTC + tc
            for w in range(4):
                bank = 3 + w % 2
                def em_at(tc=tc, w=w, bank=bank):
                    r = None
                    for j in range(4):
                        h = ATPERM[w * 4 + j]
                        po = (h % 2) * 64
                        r = nc.tensor.matmul(
                            PS[bank][:, j * 128:(j + 1) * 128],
                            KFM[po:po + 64, tc, h // 2, :],
                            QFM[po:po + 64, tc, h // 2, :],
                            start=(j == 0), stop=(j == 3))
                    return r
                op("T", em_at, ["KFM", "QFM"], [f"ps!{bank}"], name=f"at{g}.{w}")
                op("V", (lambda w=w, bank=bank: nc.vector.tensor_tensor(
                    ATS[:, w * 4:(w + 1) * 4, :], PS[bank][:, 0:512], MASK[:],
                    op=ALU.mult)),
                    [f"ps!{bank}", "MASK"], ["ATS"], name=f"ate{g}.{w}")

            groups = [list(range(0, 7)), list(range(7, 14)), [14, 15]]
            for bi, grp in enumerate(groups):
                use_inter = g > 0
                def em_nd(tc=tc, g=g, bi=bi, grp=grp, use_inter=use_inter):
                    r = None
                    for idx, h in enumerate(grp):
                        po = (h % 2) * 64
                        off = idx * (HD + 1)
                        r = nc.tensor.matmul(
                            PS[bi][:, off:off + HD + 1], ATS[:, ATSLOT[h], :],
                            VTOK[:, tc, h, :], start=(idx == 0),
                            stop=(not use_inter) and idx == len(grp) - 1)
                        if use_inter:
                            r = nc.tensor.matmul(
                                PS[bi][:, off:off + HD + 1],
                                QFM[po:po + 64, tc, h // 2, :],
                                STSB[po:po + 64, h // 2, :],
                                start=False, stop=(idx == len(grp) - 1))
                    return r
                rd = ["ATS", "VTOK", "VT1"] + (["QFM", "STSB"] if use_inter else [])
                op("T", em_nd, rd, [f"ps!{bi}"], name=f"nd{g}.{bi}")

            for sb_i in range(2):
                def em_st(tc=tc, g=g, sb_i=sb_i):
                    r = None
                    for j in range(8):
                        h = sb_i * 8 + j
                        po = (h % 2) * 64
                        slot = (h // 2) % 4
                        r = nc.tensor.matmul(
                            SPS[sb_i][po:po + 64, slot * 65:slot * 65 + 65],
                            KTOK[:, tc, h * HD:(h + 1) * HD],
                            VTOK[:, tc, h, :],
                            start=(g == 0 and j <= 1), stop=False,
                            skip_group_check=True,
                            tile_position=(0, 64) if po else None)
                    return r
                op("T", em_st, ["KTOK", "VTOK", "VT1"], [f"ps!s{sb_i}"],
                   name=f"st{g}.{sb_i}")
                op("S", (lambda sb_i=sb_i: nc.scalar.copy(
                    STSB[:, sb_i * 4:(sb_i + 1) * 4, :],
                    SPS[sb_i][:, 0:4 * 65])),
                    [f"ps!s{sb_i}"], ["STSB"], name=f"ste{g}.{sb_i}")

            for bi, grp in enumerate(groups):
                ng = len(grp)
                op("V", (lambda bi=bi, grp=grp, ng=ng: nc.vector.tensor_scalar(
                    DEN[:, grp[0]:grp[0] + ng],
                    PS[bi][:, 0:ng * 65].rearrange("p (h c) -> p h c", c=65)[:, :, 64],
                    float(EPS_DEN), None, op0=ALU.add)),
                    [f"ps!{bi}"], ["DEN"], name=f"den{g}.{bi}")
            op("V", lambda: nc.vector.reciprocal(REC[:], DEN[:]),
               ["DEN"], ["REC"], name=f"rec{g}")
            for bi, grp in enumerate(groups):
                for idx, h in enumerate(grp):
                    op("S", (lambda tc=tc, bi=bi, idx=idx, h=h:
                             nc.scalar.activation(
                                 ATN[:, tc, h * HD:(h + 1) * HD],
                                 PS[bi][:, idx * 65:idx * 65 + HD],
                                 AF.Identity, bias=f32(0.0),
                                 scale=REC[:, h:h + 1])),
                       [f"ps!{bi}", "REC"], ["ATN"], name=f"ne{g}.{h}")

        dma("trat", AFM[:], ATN[:].rearrange("p a d -> p (a d)"), ["ATN"], ["AFM"],
            transpose=True, name=f"trat{t}")

    def stD(t):
        p = t % 2
        XR = XRES[p]
        xr_res = f"XRES{p}"
        for tc in range(NTC):
            def em_o(tc=tc):
                r = None
                for kc in range(NKC):
                    st = AFM[:, tc, kc, :]
                    for hf in range(2):
                        r = nc.tensor.matmul(PS[3 + hf][:, 0:512], st,
                                             WF_["o"][:, kc, hf * 512:(hf + 1) * 512],
                                             start=(kc == 0), stop=False)
                for hf in range(2):
                    r = nc.tensor.matmul(PS[3 + hf][:, 0:512], ONES1[:],
                                         RMO[:, hf * 512:(hf + 1) * 512],
                                         start=False, stop=True)
                return r
            op("T", em_o, ["AFM", "Wo", "ONES1", "RMO0"],
               ["ps!3", "ps!4"], name=f"oproj{t}.{tc}")
            op("S", (lambda tc=tc: nc.scalar.copy(
                PCOPY[:, tc, 0:512], PS[3][:, 0:512])),
               ["ps!3"], [f"PC.{tc}"], name=f"ocp{t}.{tc}.0")
            op("S", (lambda tc=tc: nc.scalar.copy(
                PCOPY[:, tc, 512:D], PS[4][:, 0:512])),
               ["ps!4"], [f"PC.{tc}"], name=f"ocp{t}.{tc}.1")
            op("V", (lambda tc=tc: nc.vector.bn_stats(BNB[:, 0:6],
                                                      PCOPY[:, tc, 0:512])),
               [f"PC.{tc}"], ["BNB"], name=f"bs2a{t}.{tc}")
            op("V", (lambda tc=tc: nc.vector.bn_stats(BNB[:, 6:12],
                                                      PCOPY[:, tc, 512:D])),
               [f"PC.{tc}"], ["BNB"], name=f"bs2b{t}.{tc}")
            op("V", (lambda tc=tc: nc.vector.bn_aggr(AGG[:, 1, tc, :], BNB[:])),
               ["BNB"], ["AGG2"], name=f"ba2{t}.{tc}")
            op("S", (lambda tc=tc: nc.scalar.activation(ds("tmp", tc), AGG[:, 1, tc, 1:2],
                                                        AF.Ln, bias=EPS_LN)),
               ["AGG2"], ["DS.tmp"], name=f"ln2l{t}.{tc}")
            op("S", (lambda tc=tc: nc.scalar.activation(ds("r2", tc), ds("tmp", tc),
                                                        AF.Exp, bias=f32(0.0),
                                                        scale=f32(-0.5))),
               ["DS.tmp"], ["DS.r2"], name=f"ln2e{t}.{tc}")
            op("V", (lambda tc=tc: nc.vector.scalar_tensor_tensor(
                ds("tsc2", tc), AGG[:, 1, tc, 0:1], -1.0, ds("r2", tc),
                op0=ALU.mult, op1=ALU.mult)),
               ["AGG2", "DS.r2"], ["DS.tsc2"], name=f"tsc2{t}.{tc}")
            op("S", (lambda tc=tc, XR=XR: nc.scalar.activation(
                WBUF[:, tc * D:(tc + 1) * D], XR[:, tc, :], AF.Identity,
                bias=ds("tsc2", tc))),
               [f"{xr_res}.{tc}", "DS.tsc2"], [f"WB.{tc}"], name=f"xmod{t}.{tc}")
            op("V", (lambda tc=tc: nc.vector.scalar_tensor_tensor(
                X2[:, tc, :], PCOPY[:, tc, :], ds("r2", tc),
                WBUF[:, tc * D:(tc + 1) * D],
                op0=ALU.mult, op1=ALU.add)),
                [f"PC.{tc}", "DS.r2", f"WB.{tc}"], ["X2"],
                name=f"x2{t}.{tc}")

    def stE(t):
        p = t % 2
        XR = XRES[p]
        xr_res = f"XRES{p}"
        XRflat = XR[:].rearrange("p a d -> p (a d)")
        for tc in range(NTC):
            op("V", (lambda tc=tc: nc.vector.bn_stats(BNB[:, 0:6], X2[:, tc, 0:512])),
               ["X2"], ["BNB"], name=f"bs3a{t}.{tc}")
            op("V", (lambda tc=tc: nc.vector.bn_stats(BNB[:, 6:12], X2[:, tc, 512:D])),
               ["X2"], ["BNB"], name=f"bs3b{t}.{tc}")
            op("V", (lambda tc=tc: nc.vector.bn_aggr(AGG[:, 2, tc, :], BNB[:])),
               ["BNB"], ["AGG3"], name=f"ba3{t}.{tc}")
        op("S", lambda: nc.scalar.activation(ds("tmp"), AGG[:, 2, :, 1], AF.Ln,
                                             bias=EPS_LN),
           ["AGG3"], ["DS.tmp"], name=f"ln3l{t}")
        op("S", lambda: nc.scalar.activation(ds("r3"), ds("tmp"), AF.Exp,
                                             bias=f32(0.0), scale=f32(-0.5)),
           ["DS.tmp"], ["DS.r3"], name=f"ln3e{t}")
        op("V", lambda: nc.vector.tensor_scalar(ds("nm3"), AGG[:, 2, :, 0], -1.0,
                                                None, op0=ALU.mult),
           ["AGG3"], ["DS.nm3"], name=f"nm3{t}")
        for tc in range(NTC):
            op("S", (lambda tc=tc: nc.scalar.activation(
                EQ[:, tc * D:(tc + 1) * D], X2[:, tc, :], AF.Abs,
                bias=ds("nm3", tc))),
                ["X2", "DS.nm3"], ["EQ"], name=f"absd{t}.{tc}")
            op("V", (lambda tc=tc: nc.vector.tensor_reduce(
                ds("amx", tc), EQ[:, tc * D:(tc + 1) * D], axis=AX.X, op=ALU.max)),
               ["EQ"], ["DS.amx"], name=f"amx{t}.{tc}")
        op("V", lambda: nc.vector.tensor_tensor(ds("tmp2"), ds("amx"), ds("r3"),
                                                op=ALU.mult),
           ["DS.amx", "DS.r3"], ["DS.tmp2"], name=f"aml{t}")
        op("V", lambda: nc.vector.tensor_scalar(ds("xs"), ds("tmp2"), float(EPS_Q),
                                                1.0 / QMAX, op0=ALU.max,
                                                op1=ALU.mult),
           ["DS.tmp2"], ["DS.xs"], name=f"xs{t}")
        op("V", lambda: nc.vector.reciprocal(ds("sinv"), ds("xs")),
           ["DS.xs"], ["DS.sinv"], name=f"sinv{t}")
        op("V", lambda: nc.vector.tensor_scalar(ds("nsinv"), ds("sinv"), -1.0,
                                                None, op0=ALU.mult),
           ["DS.sinv"], ["DS.nsinv"], name=f"nsinv{t}")
        op("V", lambda: nc.vector.tensor_tensor(ds("rs"), ds("r3"), ds("sinv"),
                                                op=ALU.mult),
           ["DS.r3", "DS.sinv"], ["DS.rs"], name=f"rs{t}")
        op("V", lambda: nc.vector.scalar_tensor_tensor(ds("tmp2"), ds("nm3"),
                                                       1.0, ds("rs"),
                                                       op0=ALU.mult, op1=ALU.mult),
           ["DS.nm3", "DS.rs"], ["DS.tmp2"], name=f"s2a{t}")
        op("V", lambda: nc.vector.tensor_scalar(ds("s2"), ds("tmp2"), 0.5,
                                                None, op0=ALU.add),
           ["DS.tmp2"], ["DS.s2"], name=f"s2{t}")
        op("V", lambda: nc.vector.tensor_scalar(ds("nrs"), ds("rs"), -1.0,
                                                None, op0=ALU.mult),
           ["DS.rs"], ["DS.nrs"], name=f"nrs{t}")
        op("V", lambda: nc.vector.tensor_scalar(ds("s2n"), ds("s2"), 127.5, -1.0,
                                                op0=ALU.add, op1=ALU.mult),
           ["DS.s2"], ["DS.s2n"], name=f"s2n{t}")

    def stEb(t):
        p = t % 2
        XR = XRES[p]
        xr_res = f"XRES{p}"
        XRflat = XR[:].rearrange("p a d -> p (a d)")
        for tc in range(NTC):
            op("S", (lambda tc=tc: nc.scalar.activation(
                WBUF[:, tc * D:(tc + 1) * D], X2[:, tc, :], AF.Identity,
                bias=ds("s2", tc), scale=ds("rs", tc))),
                ["X2", "DS.s2", "DS.rs"], [f"WB.{tc}"], name=f"pw{t}.{tc}")
        for k in range(8):
            half = float(2 ** (6 - k))
            for tc in range(NTC):
                dsl = slice(tc * D, (tc + 1) * D)
                vin = WBUF[:, dsl] if k % 2 == 0 else XRflat[:, dsl]
                vout = XRflat[:, dsl] if k % 2 == 0 else WBUF[:, dsl]
                vin_r = f"WB.{tc}" if k % 2 == 0 else f"{xr_res}.{tc}"
                vout_r = f"{xr_res}.{tc}" if k % 2 == 0 else f"WB.{tc}"
                op("S", (lambda vin=vin, dsl=dsl: nc.scalar.activation(
                    SGB[:, dsl], vin, AF.Sign)),
                   [vin_r], [f"SG.{tc}"], name=f"qsg{t}.{k}.{tc}")
                op("V", (lambda vin=vin, vout=vout, half=half, dsl=dsl:
                         nc.vector.scalar_tensor_tensor(vout, SGB[:, dsl], -half,
                                                        vin, op0=ALU.mult,
                                                        op1=ALU.add)),
                   [f"SG.{tc}", vin_r], [vout_r], name=f"qvs{t}.{k}.{tc}")
        for tc in range(NTC):
            op("V", (lambda tc=tc, XRflat=XRflat: nc.vector.scalar_tensor_tensor(
                XRflat[:, tc * D:(tc + 1) * D], X2[:, tc, :], ds("nrs", tc),
                WBUF[:, tc * D:(tc + 1) * D], op0=ALU.mult, op1=ALU.add)),
               ["X2", "DS.nrs", f"WB.{tc}"], [f"{xr_res}.{tc}"], name=f"xqa{t}.{tc}")
            op("S", (lambda tc=tc, XRflat=XRflat: nc.scalar.activation(
                XQT[:, tc * D:(tc + 1) * D], XRflat[:, tc * D:(tc + 1) * D],
                AF.Identity, bias=ds("s2n", tc))),
               [f"{xr_res}.{tc}", "DS.s2n"], ["XRT"], name=f"xqb{t}.{tc}")
        dma("trxq", XQFM[:], XQT[:], ["XRT"], ["XQFM"], transpose=True,
            name=f"trxq{t}")

    def stE2(t):
        for tc in range(NTC):
            op("T", (lambda tc=tc: nc.tensor.transpose(
                PS[5][0:1, 384:512], ds("nsinv", tc), IDENT[:])),
                ["DS.nsinv", "IDENT"], ["ps!5"], name=f"t3{t}.{tc}")
            op("S", (lambda tc=tc: nc.scalar.copy(
                ROWF[0:1, tc, :], PS[5][0:1, 384:512])),
                ["ps!5"], ["ROWF0"], name=f"t3e{t}.{tc}")

    def stF(t):
        p = t % 2
        XR = XRES[p]
        xr_res = f"XRES{p}"
        for tc in range(NTC):
            def em_f(tc=tc):
                r = None
                for kc in range(NKC):
                    st = XQFM[:, tc, kc, :]
                    for hf in range(2):
                        r = nc.tensor.matmul(PS[3 + hf][:, 0:512], st,
                                             WDQF[:, kc, hf * 512:(hf + 1) * 512],
                                             start=(kc == 0), stop=False)
                for hf in range(2):
                    r = nc.tensor.matmul(PS[3 + hf][:, 0:512], ONES1[:],
                                         RMFA[:, hf * 512:(hf + 1) * 512],
                                         start=False, stop=False)
                    r = nc.tensor.matmul(PS[3 + hf][:, 0:512], ROWF[0:1, tc, :],
                                         RM["f"][0:1, hf * 512:(hf + 1) * 512],
                                         start=False, stop=True)
                return r
            op("T", em_f, ["XQFM", "WDQF", "ROWF0", "RMf0", "RMFA", "ONES1"],
               ["ps!3", "ps!4"], name=f"ffn{t}.{tc}")
            op("S", (lambda tc=tc: nc.scalar.copy(
                PCOPY[:, tc, 0:512], PS[3][:, 0:512])),
               ["ps!3"], [f"PC.{tc}"], name=f"fcp{t}.{tc}.0")
            op("S", (lambda tc=tc: nc.scalar.copy(
                PCOPY[:, tc, 512:D], PS[4][:, 0:512])),
               ["ps!4"], [f"PC.{tc}"], name=f"fcp{t}.{tc}.1")
            op("V", (lambda tc=tc: nc.vector.bn_stats(BNB[:, 0:6],
                                                      PCOPY[:, tc, 0:512])),
               [f"PC.{tc}"], ["BNB"], name=f"bs4a{t}.{tc}")
            op("V", (lambda tc=tc: nc.vector.bn_stats(BNB[:, 6:12],
                                                      PCOPY[:, tc, 512:D])),
               [f"PC.{tc}"], ["BNB"], name=f"bs4b{t}.{tc}")
            op("V", (lambda tc=tc: nc.vector.bn_aggr(AGG[:, 3, tc, :], BNB[:])),
               ["BNB"], ["AGG4"], name=f"ba4{t}.{tc}")
            op("V", (lambda tc=tc: nc.vector.tensor_tensor(
                ds("xs2", tc), ds("xs", tc), ds("xs", tc), op=ALU.mult)),
               ["DS.xs"], ["DS.xs2"], name=f"xs2{t}.{tc}")
            op("V", (lambda tc=tc: nc.vector.tensor_tensor(
                ds("vf", tc), AGG[:, 3, tc, 1:2], ds("xs2", tc), op=ALU.mult)),
               ["AGG4", "DS.xs2"], ["DS.vf"], name=f"vf{t}.{tc}")
            op("S", (lambda tc=tc: nc.scalar.activation(ds("tmp", tc), ds("vf", tc),
                                                        AF.Ln, bias=EPS_LN)),
               ["DS.vf"], ["DS.tmp"], name=f"ln4l{t}.{tc}")
            op("S", (lambda tc=tc: nc.scalar.activation(ds("r4", tc), ds("tmp", tc),
                                                        AF.Exp, bias=f32(0.0),
                                                        scale=f32(-0.5))),
               ["DS.tmp"], ["DS.r4"], name=f"ln4e{t}.{tc}")
            op("V", (lambda tc=tc: nc.vector.scalar_tensor_tensor(
                ds("sfin", tc), ds("xs", tc), -1.0, ds("r4", tc),
                op0=ALU.mult, op1=ALU.mult)),
               ["DS.xs", "DS.r4"], ["DS.sfin"], name=f"sfin{t}.{tc}")
            op("V", (lambda tc=tc: nc.vector.tensor_scalar(
                ds("nmp", tc), AGG[:, 3, tc, 0:1], -1.0, None, op0=ALU.mult)),
               ["AGG4"], ["DS.nmp"], name=f"nmp{t}.{tc}")
            op("V", (lambda tc=tc: nc.vector.tensor_tensor(
                ds("tscf", tc), ds("nmp", tc), ds("sfin", tc), op=ALU.mult)),
               ["DS.nmp", "DS.sfin"], ["DS.tscf"], name=f"tscf{t}.{tc}")
            op("S", (lambda tc=tc: nc.scalar.activation(
                WBUF[:, tc * D:(tc + 1) * D], X2[:, tc, :], AF.Identity,
                bias=ds("tscf", tc))),
               ["X2", "DS.tscf"], [f"WB.{tc}"], name=f"fmod{t}.{tc}")
            op("V", (lambda tc=tc, XR=XR: nc.vector.scalar_tensor_tensor(
                XR[:, tc, :], PCOPY[:, tc, :], ds("sfin", tc),
                WBUF[:, tc * D:(tc + 1) * D],
                op0=ALU.mult, op1=ALU.add)),
                [f"PC.{tc}", "DS.sfin", f"WB.{tc}"], [f"{xr_res}.{tc}"],
                name=f"fin{t}.{tc}")
        dst = out_h[t * TT:(t + 1) * TT, :].rearrange("(a b) d -> b a d", a=NTC)
        dma(f"out{p}", dst, XR[:], [f"{xr_res}.0", f"{xr_res}.1"], [f"OUT{t}"],
            name=f"store{t}")

    stA(0)
    stA2(0)
    stB(0)
    st_wquant()
    stC(0)
    for t in range(NTILE):
        stD(t)
        if t + 1 < NTILE:
            stA(t + 1)
            stA2(t + 1)
            stB(t + 1)
        stE(t)
        stE2(t)
        stEb(t)
        if t + 1 < NTILE:
            stC(t + 1)
        stF(t)

    sems = {}
    for name in sc.counts:
        sems[name] = es.enter_context(nc.semaphore(name=f"sem_{name}".replace(".", "_")))

    by_engine = {}
    for o in sc.ops:
        by_engine.setdefault(o["engine"], []).append(o)

    def run_ops(proxy, key):
        for o in by_engine.get(key, ()):
            for s, v in o["waits"]:
                proxy.wait_ge(sems[s], v)
            r = o["emit"]()
            if isinstance(r, (list, tuple)):
                r = r[-1]
            assert r is not None, f"op {o['name']} returned no instruction"
            r.then_inc(sems[o["sem"]], o["step"])

    with nc.Block() as blk:
        @blk.sync
        def _(q):
            run_ops(q, "Q")
            for s in sc.dma_sems:
                q.wait_ge(sems[s], sc.counts[s])

        @blk.gpsimd
        def _(gp):
            run_ops(gp, "G")

        @blk.tensor
        def _(tt_):
            run_ops(tt_, "T")

        @blk.vector
        def _(v):
            run_ops(v, "V")

        @blk.scalar
        def _(s_):
            run_ops(s_, "S")

    es.close()
    return nc


_NC_CACHE = {}


def _consts():
    mask = np.triu(np.ones((128, 128), dtype=np.float32))
    maskrep = np.tile(mask, (1, 4)).astype(ml_dtypes.bfloat16)
    ident = np.eye(128, dtype=np.float32)
    return maskrep, ident


def kernel(L0_attn_x, L0_attn_mask, L0_attn_Wq, L0_attn_bq, L0_attn_Wk,
           L0_attn_bk, L0_attn_Wv, L0_attn_bv, L0_attn_Wo, L0_attn_bo,
           L0_ffn_weight, L0_ffn_bias):
    x = np.ascontiguousarray(np.asarray(L0_attn_x, dtype=np.float32))
    B, S, D_ = x.shape
    assert D_ == D
    maskrep, ident = _consts()
    shared = {
        "wq": np.ascontiguousarray(np.asarray(L0_attn_Wq, dtype=np.float32)),
        "wk": np.ascontiguousarray(np.asarray(L0_attn_Wk, dtype=np.float32)),
        "wv": np.ascontiguousarray(np.asarray(L0_attn_Wv, dtype=np.float32)),
        "wo": np.ascontiguousarray(np.asarray(L0_attn_Wo, dtype=np.float32)),
        "wf": np.ascontiguousarray(np.asarray(L0_ffn_weight, dtype=np.float32)),
        "bq": np.ascontiguousarray(np.asarray(L0_attn_bq, dtype=np.float32)),
        "bk": np.ascontiguousarray(np.asarray(L0_attn_bk, dtype=np.float32)),
        "bv": np.ascontiguousarray(np.asarray(L0_attn_bv, dtype=np.float32)),
        "bo": np.ascontiguousarray(np.asarray(L0_attn_bo, dtype=np.float32)),
        "bf": np.ascontiguousarray(np.asarray(L0_ffn_bias, dtype=np.float32)),
        "maskc": maskrep,
        "identc": ident,
    }
    key = S
    if key not in _NC_CACHE:
        _NC_CACHE[key] = build_program(S)
    nc = _NC_CACHE[key]
    in_maps = [dict(shared, x=np.ascontiguousarray(x[b])) for b in range(B)]
    res = run_bass_kernel_spmd(nc, in_maps, core_ids=list(range(B)))
    out = np.stack([np.asarray(r["out"]) for r in res.results], axis=0)
    return out.astype(np.float32)


# version 13
# speedup vs baseline: 1.1952x; 1.1952x over previous
import numpy as np
import ml_dtypes
from contextlib import ExitStack

import concourse.bass as bass
import concourse.mybir as mybir
from concourse.bass_utils import run_bass_kernel_spmd
import concourse.bass_utils as _bu

if not getattr(_bu, "_ldw_opt_patched", False):
    _orig_gwa = _bu.get_walrus_args
    def _gwa_ldw(*a, **k):
        return [x.replace("--enable-ldw-opt=false", "--enable-ldw-opt=true")
                for x in _orig_gwa(*a, **k)]
    _bu.get_walrus_args = _gwa_ldw
    _bu._ldw_opt_patched = True

F32 = mybir.dt.float32
BF16 = mybir.dt.bfloat16
AF = mybir.ActivationFunctionType
ALU = mybir.AluOpType
AX = mybir.AxisListType
f32 = np.float32

D = 1024
H = 16
HD = 64
NKC = 8
TT = 256
NTC = 2
EPS_LN = f32(1e-5)
EPS_DEN = f32(1e-6)
EPS_Q = f32(1e-8)
QMAX = 127.0


class Sched:

    def __init__(self):
        self.ops = []
        self.counts = {}
        self.res_w = {}
        self.res_r = {}
        self.excl = {}
        self.sem_step = {}
        self.dma_sems = []
        self.last_transpose = None
        self.watermark = {}

    def sem(self, name, step):
        if name not in self.counts:
            self.counts[name] = 0
            self.sem_step[name] = step
            if step == 16:
                self.dma_sems.append(name)

    def add(self, engine, emit, reads=(), writes=(), sem=None, step=1,
            is_transpose=False, is_dma=False, name="", extra_waits=()):
        if sem is None:
            sem = engine
        self.sem(sem, step)
        need = {}

        def req(dep):
            if dep is None:
                return
            s, c = dep
            if self.sem_step.get(s) == 16:
                c = self.counts[s]
            if c <= self.watermark.get((engine, s), 0):
                return
            if need.get(s, 0) < c:
                need[s] = c

        for r in reads:
            req(self.excl.get(r) if r.startswith("ps!") else self.res_w.get(r))
        for w in writes:
            if w.startswith("ps!"):
                req(self.excl.get(w))
            else:
                req(self.res_w.get(w))
                for s, c in self.res_r.get(w, {}).items():
                    req((s, c))
        if is_transpose:
            for s in self.dma_sems:
                if self.counts[s]:
                    req((s, self.counts[s]))
        if is_dma and not is_transpose and self.last_transpose is not None:
            req(self.last_transpose)
        for d in extra_waits:
            req(d)

        self.counts[sem] += step
        cnt = self.counts[sem]
        for s, c in need.items():
            self.watermark[(engine, s)] = max(self.watermark.get((engine, s), 0), c)
        self.ops.append(dict(engine=engine, emit=emit, waits=sorted(need.items()),
                             sem=sem, step=step, name=name))
        tag = (sem, cnt)
        for r in reads:
            if r.startswith("ps!"):
                self.excl[r] = tag
            else:
                self.res_r.setdefault(r, {})[sem] = cnt
        for w in writes:
            if w.startswith("ps!"):
                self.excl[w] = tag
            else:
                self.res_w[w] = tag
                self.res_r[w] = {}
        if is_transpose:
            self.last_transpose = tag
        return tag


def build_program(S=4096, phases=99, asub=9):
    NTILE = S // TT
    nc = bass.Bass("TRN2", target_bir_lowering=False, debug=False)

    x_h = nc.dram_tensor("x", [S, D], F32, kind="ExternalInput").ap()
    out_h = nc.dram_tensor("out", [S, D], F32, kind="ExternalOutput").ap()
    w_h = {}
    b_h = {}
    for n in ("q", "k", "v", "o", "f"):
        w_h[n] = nc.dram_tensor(f"w{n}", [D, D], F32, kind="ExternalInput").ap()
        b_h[n] = nc.dram_tensor(f"b{n}", [D], F32, kind="ExternalInput").ap()
    mask_h = nc.dram_tensor("maskc", [128, 512], BF16, kind="ExternalInput").ap()
    id_h = nc.dram_tensor("identc", [128, 128], F32, kind="ExternalInput").ap()

    es = ExitStack()
    sbuf = lambda n, shp, dt: es.enter_context(nc.sbuf_tensor(n, shp, dt))
    psum = lambda n: es.enter_context(nc.psum_tensor(n, [128, 512], F32))

    XRES = [sbuf(f"xres{p}", [128, NTC, D], F32) for p in range(2)]
    X2 = sbuf("x2", [128, NTC, D], F32)
    WBUF = sbuf("wbuf", [128, NTC * D], F32)
    XRFM = sbuf("xrfm", [128, NTC, NKC, 128], BF16)
    QTOK = sbuf("qtok", [128, NTC * D], BF16)
    KTOK = sbuf("ktok", [128, NTC, D], BF16)
    QFM = sbuf("qfm", [128, NTC, NKC, 128], BF16)
    KFM = sbuf("kfm", [128, NTC, NKC, 128], BF16)
    VTOK = sbuf("vtok", [128, NTC, H, HD + 1], BF16)
    EQ = sbuf("eq", [128, NTC * D], BF16)
    XRT = EQ
    PQ = sbuf("pq", [128, NTC * D], BF16)
    EK = sbuf("ek", [128, NTC * D], BF16)
    PK = sbuf("pk", [128, NTC * D], BF16)
    SGB = sbuf("sgb", [128, NTC * D], BF16)
    PCOPY = sbuf("pcopy", [128, NTC, D], F32)
    ATS = sbuf("ats", [128, NTC, H, 128], BF16)
    ATN = sbuf("atn", [128, NTC, D], BF16)
    AFM = sbuf("afm", [128, NTC, NKC, 128], BF16)
    XQFM = sbuf("xqfm", [128, NTC, NKC, 128], BF16)
    XQT = XRT
    DEN = sbuf("den", [128, H], F32)
    REC = sbuf("rec", [128, H], F32)
    BNB = sbuf("bnb", [128, 12], F32)
    AGG = sbuf("agg", [128, 4, NTC, 2], F32)
    DS = sbuf("ds", [128, 48], F32)
    STSB = sbuf("stsb", [128, H // 2, HD + 1], BF16)
    ROWQ = sbuf("rowq", [2, NTC, 128], BF16)
    ROWF = sbuf("rowf", [1, NTC, 128], BF16)
    RMFA = sbuf("rmfa", [1, D], BF16)
    ROWP = sbuf("rowp", [1, NTC, 128], BF16)
    RM = {n: sbuf(f"rm{n}", [2, D], BF16) for n in ("q", "k", "v", "f")}
    RMO = sbuf("rmo", [1, D], BF16)
    ONES1 = sbuf("ones1", [1, 128], BF16)
    ONESD = sbuf("onesd", [1, D], BF16)
    ONESC = sbuf("onesc", [128, 1], BF16)
    MASK = sbuf("maskt", [128, 512], BF16)
    IDENT = sbuf("identt", [128, 128], F32)
    WF_ = {n: sbuf(f"w{n}f", [128, NKC, D], BF16) for n in ("q", "k", "v", "o")}
    WDQF = sbuf("wdqf", [128, NKC, D], BF16)
    WSC = sbuf("wsc", [128, 6], F32)
    WNAT = PCOPY[:, 0, :]
    WDQN = SGB[:, D:2 * D]

    PS = [psum(f"ps{i}") for i in range(6)]
    SPS = [psum(f"sps{i}") for i in range(2)]

    sc = Sched()

    DSC = dict(r1=0, nmr=2, tmp=4, r2=6, nm2=8, r3=10, nm3=12, amx=14,
               xs=16, sinv=18, nsinv=20, rs=22, s2=24, r4=26, sfin=28,
               nmp=30, xs2=32, vf=34, tmp2=36, nrs=38, s2n=40, tsc2=42, tscf=44)

    def ds(k, tc=None):
        c = DSC[k]
        return DS[:, c:c + 2] if tc is None else DS[:, c + tc:c + tc + 1]

    def dma(stream, dst, src, reads, writes, transpose=False, name=""):
        s = f"Q.{stream}"
        sc.sem(s, 16)
        return sc.add("Q", lambda: nc.sync.dma_start(dst, src, transpose=transpose),
                      reads=reads, writes=writes, sem=s, step=16,
                      is_transpose=transpose, is_dma=True, name=name)

    def gdma(dst, src, reads, writes, name=""):
        sc.sem("Gd", 16)
        ew = [("Gd", sc.counts["Gd"])] if sc.counts["Gd"] else []
        return sc.add("G", lambda: nc.gpsimd.dma_start(dst, src),
                      reads=reads, writes=writes, sem="Gd", step=16,
                      is_dma=True, name=name, extra_waits=ew)

    op = sc.add

    dma("ld", MASK[:], mask_h[:], [], ["MASK"], name="mask")
    dma("ld", IDENT[:], id_h[:], [], ["IDENT"], name="ident")
    op("V", lambda: nc.vector.memset(ONES1[:], 1.0), [], ["ONES1"], name="ms1")
    op("V", lambda: nc.vector.memset(ONESD[:], 1.0), [], ["ONESD"], name="msD")
    op("V", lambda: nc.vector.memset(ONESC[:], 1.0), [], ["ONESC"], name="msC")
    op("V", lambda: nc.vector.memset(ROWQ[0:2], 1.0), [], ["ROWQ1"], name="msRQ")
    op("V", lambda: nc.vector.memset(VTOK[:, :, :, HD:HD + 1], 1.0), [], ["VT1"],
       name="msVT")

    for n in ("q", "k", "v", "o"):
        gdma(WF_[n][:], w_h[n].rearrange("(a b) d -> b a d", a=NKC),
             [], [f"W{n}"], name=f"w{n}cast")
    for n in ("q", "k", "v"):
        gdma(RM[n][1:2, :], b_h[n].rearrange("(a d) -> a d", a=1), [], [f"RM{n}1"],
             name=f"b{n}")
    gdma(RMO[0:1, :], b_h["o"].rearrange("(a d) -> a d", a=1), [], ["RMO0"], name="bo")
    gdma(RM["f"][0:1, :], b_h["f"].rearrange("(a d) -> a d", a=1), [], ["RMf0"],
         name="bfn")

    for n in ("q", "k", "v"):
        for hf in range(2):
            def em_w1(n=n, hf=hf):
                r = None
                for kc in range(NKC):
                    r = nc.tensor.matmul(PS[4][0:1, 0:512], ONESC[:],
                                         WF_[n][:, kc, hf * 512:(hf + 1) * 512],
                                         start=(kc == 0), stop=(kc == NKC - 1))
                return r
            op("T", em_w1, [f"W{n}", "ONESC"], ["ps!4"], name=f"w1{n}{hf}")
            op("V", (lambda n=n, hf=hf: nc.vector.tensor_copy(
                RM[n][0:1, hf * 512:(hf + 1) * 512], PS[4][0:1, 0:512])),
                ["ps!4"], [f"RM{n}0"], name=f"w1e{n}{hf}")

    def st_wquant():
        for oc in range(NKC):
            WN = WNAT[:, (oc % 2) * D:(oc % 2) * D + D] if False else WNAT[:]
            dma("wfl", WNAT, w_h["f"][oc * 128:(oc + 1) * 128, :], ["PC.0"], ["PC.0"],
                name=f"wf{oc}")
            op("V", lambda: nc.vector.tensor_reduce(WSC[:, 0:1], WNAT, axis=AX.X,
                                                    op=ALU.max, apply_absolute_value=True),
               ["PC.0"], ["WSC"], name="wam")
            op("V", lambda: nc.vector.tensor_scalar(WSC[:, 1:2], WSC[:, 0:1], float(EPS_Q),
                                                    1.0 / QMAX, op0=ALU.max,
                                                    op1=ALU.mult),
               ["WSC"], ["WSC"], name="wsc")
            op("V", lambda: nc.vector.reciprocal(WSC[:, 2:3], WSC[:, 1:2]),
               ["WSC"], ["WSC"], name="wsi")
            op("V", lambda: nc.vector.tensor_scalar(WSC[:, 3:4], WSC[:, 1:2], -1.0,
                                                    None, op0=ALU.mult),
               ["WSC"], ["WSC"], name="wns")
            op("V", lambda: nc.vector.tensor_scalar(WSC[:, 4:5], WSC[:, 3:4], 128.0,
                                                    None, op0=ALU.mult),
               ["WSC"], ["WSC"], name="wnb")
            op("S", lambda: nc.scalar.activation(WBUF[:, 0:D], WNAT, AF.Identity,
                                                 bias=f32(0.5), scale=WSC[:, 2:3]),
               ["PC.0", "WSC"], ["WB.0"], name="wu")
            for k in range(8):
                half = float(2 ** (6 - k))
                vin = WBUF[:, 0:D] if k == 0 else (PCOPY[:, 1, :] if k % 2 == 1 else WBUF[:, D:2 * D])
                vout = PCOPY[:, 1, :] if k % 2 == 0 else WBUF[:, D:2 * D]
                vin_r = "WB.0" if k == 0 else ("PC.1" if k % 2 == 1 else "WB.1")
                vout_r = "PC.1" if k % 2 == 0 else "WB.1"
                op("S", (lambda vin=vin: nc.scalar.activation(SGB[:, 0:D], vin, AF.Sign)),
                   [vin_r], ["SG.0"], name=f"wsg{k}")
                op("V", (lambda vin=vin, vout=vout, half=half:
                         nc.vector.scalar_tensor_tensor(vout, SGB[:, 0:D], -half, vin,
                                                        op0=ALU.mult, op1=ALU.add)),
                   ["SG.0", vin_r], [vout_r], name=f"wvs{k}")
            op("V", lambda: nc.vector.scalar_tensor_tensor(PCOPY[:, 1, :], WBUF[:, D:2 * D],
                                                           -127.5, WBUF[:, 0:D],
                                                           op0=ALU.add, op1=ALU.subtract),
               ["WB.1", "WB.0"], ["PC.1"], name="wfneg")
            op("V", lambda: nc.vector.tensor_scalar(WDQN[:], PCOPY[:, 1, :], WSC[:, 3:4],
                                                    WSC[:, 4:5], op0=ALU.mult, op1=ALU.add),
               ["PC.1", "WSC"], ["SG.1"], name="wdq")
            dma("trw", WDQF[:, :, oc * 128:(oc + 1) * 128], WDQN, ["SG.1"], ["WDQF"],
                transpose=True, name=f"trw{oc}")

        for hf in range(2):
            def em_w1f(hf=hf):
                r = None
                for kc in range(NKC):
                    r = nc.tensor.matmul(PS[4][0:1, 0:512], ONESC[:],
                                         WDQF[:, kc, hf * 512:(hf + 1) * 512],
                                         start=(kc == 0), stop=(kc == NKC - 1))
                return r
            op("T", em_w1f, ["WDQF", "ONESC"], ["ps!4"], name=f"w1f{hf}")
            op("V", (lambda hf=hf: nc.vector.tensor_scalar(
                RMFA[0:1, hf * 512:(hf + 1) * 512], PS[4][0:1, 0:512], 128.0,
                None, op0=ALU.mult)),
                ["ps!4"], ["RMFA"], name=f"w1fe{hf}")


    ATPERM = [0, 2, 4, 6, 8, 10, 12, 14, 1, 3, 5, 7, 9, 11, 13, 15]
    ATSLOT = {h: i for i, h in enumerate(ATPERM)}

    def stA(t):
        p = t % 2
        XR = XRES[p]
        xr_res = f"XRES{p}"
        src = x_h[t * TT:(t + 1) * TT, :].rearrange("(a b) d -> b a d", a=NTC)
        dma(f"xl{p}", XR[:], src, [f"{xr_res}.0", f"{xr_res}.1"],
            [f"{xr_res}.0", f"{xr_res}.1"], name=f"xload{t}")
        for tc in range(NTC):
            op("V", (lambda tc=tc, XR=XR: nc.vector.bn_stats(BNB[:, 0:6], XR[:, tc, 0:512])),
               [f"{xr_res}.{tc}"], ["BNB"], name=f"bs1a{t}.{tc}")
            op("V", (lambda tc=tc, XR=XR: nc.vector.bn_stats(BNB[:, 6:12], XR[:, tc, 512:D])),
               [f"{xr_res}.{tc}"], ["BNB"], name=f"bs1b{t}.{tc}")
            op("V", (lambda tc=tc: nc.vector.bn_aggr(AGG[:, 0, tc, :], BNB[:])),
               ["BNB"], ["AGG1"], name=f"ba1{t}.{tc}")
        op("S", lambda: nc.scalar.activation(ds("tmp"), AGG[:, 0, :, 1], AF.Ln,
                                             bias=EPS_LN),
           ["AGG1"], ["DS.tmp"], name=f"ln1l{t}")
        op("S", lambda: nc.scalar.activation(ds("r1"), ds("tmp"), AF.Exp,
                                             bias=f32(0.0), scale=f32(-0.5)),
           ["DS.tmp"], ["DS.r1"], name=f"ln1e{t}")
        op("V", lambda: nc.vector.scalar_tensor_tensor(ds("nmr"), AGG[:, 0, :, 0],
                                                       -1.0, ds("r1"),
                                                       op0=ALU.mult, op1=ALU.mult),
           ["AGG1", "DS.r1"], ["DS.nmr"], name=f"nmr{t}")
        for tc in range(NTC):
            op("S", (lambda tc=tc, XR=XR: nc.scalar.activation(
                XRT[:, tc * D:(tc + 1) * D], XR[:, tc, :], AF.Identity,
                bias=f32(0.0), scale=ds("r1", tc))),
                [f"{xr_res}.{tc}", "DS.r1"], ["EQ"], name=f"xr{t}.{tc}")
        dma("trxr", XRFM[:], XRT[:], ["EQ"], ["XRFM"], transpose=True,
            name=f"trxr{t}")

    def stA2(t):
        for tc in range(NTC):
            op("T", (lambda tc=tc: nc.tensor.transpose(
                PS[5][0:1, 384:512], ds("nmr", tc), IDENT[:])),
                ["DS.nmr", "IDENT"], ["ps!5"], name=f"t1{t}.{tc}")
            op("S", (lambda tc=tc: nc.scalar.copy(
                ROWQ[0:1, tc, :], PS[5][0:1, 384:512])),
                ["ps!5"], ["ROWQ0"], name=f"t1e{t}.{tc}")

    def stB(t):
        for tc in range(NTC):
            for hf in range(2):
                def em_qkv(tc=tc, hf=hf):
                    r = None
                    for kc in range(NKC):
                        st = XRFM[:, tc, kc, :]
                        for i, w in enumerate((WF_["q"], WF_["k"], WF_["v"])):
                            r = nc.tensor.matmul(
                                PS[i][:, 0:512], st,
                                w[:, kc, hf * 512:(hf + 1) * 512],
                                start=(kc == 0), stop=False)
                    for i, n in enumerate(("q", "k", "v")):
                        r = nc.tensor.matmul(
                            PS[i][:, 0:512], ROWQ[:, tc, :],
                            RM[n][:, hf * 512:(hf + 1) * 512],
                            start=False, stop=True)
                    return r
                op("T", em_qkv,
                   ["XRFM", "ROWQ0", "ROWQ1", "Wq", "Wk", "Wv", "RMq0", "RMq1",
                    "RMk0", "RMk1", "RMv0", "RMv1"],
                   ["ps!0", "ps!1", "ps!2"], name=f"qkv{t}.{tc}.{hf}")
                sl = slice((tc * 2 + hf) * 512, (tc * 2 + hf + 1) * 512)
                op("S", (lambda sl=sl: nc.scalar.activation(
                    EQ[:, sl], PS[0][:, 0:512], AF.Exp)),
                    ["ps!0"], ["EQ"], name=f"eq{t}.{tc}.{hf}")
                op("S", (lambda sl=sl: nc.scalar.activation(
                    PQ[:, sl], PS[0][:, 0:512], AF.Relu)),
                    ["ps!0"], ["PQ"], name=f"pq{t}.{tc}.{hf}")
                op("S", (lambda sl=sl: nc.scalar.activation(
                    EK[:, sl], PS[1][:, 0:512], AF.Exp)),
                    ["ps!1"], ["EK"], name=f"ek{t}.{tc}.{hf}")
                op("S", (lambda sl=sl: nc.scalar.activation(
                    PK[:, sl], PS[1][:, 0:512], AF.Relu)),
                    ["ps!1"], ["PK"], name=f"pk{t}.{tc}.{hf}")
                op("S", (lambda tc=tc, hf=hf: nc.scalar.activation(
                    VTOK[:, tc, hf * 8:(hf + 1) * 8, 0:HD], PS[2][:, 0:512],
                    AF.Identity)),
                    ["ps!2"], ["VTOK"], name=f"ve{t}.{tc}.{hf}")
        op("V", lambda: nc.vector.scalar_tensor_tensor(QTOK[:], EQ[:], 1.0,
                                                       PQ[:], op0=ALU.min,
                                                       op1=ALU.add),
           ["EQ", "PQ"], ["QTOK"], name=f"qel{t}")
        op("V", lambda: nc.vector.scalar_tensor_tensor(
            KTOK[:].rearrange("p a d -> p (a d)"), EK[:], 1.0, PK[:],
            op0=ALU.min, op1=ALU.add),
           ["EK", "PK"], ["KTOK"], name=f"kel{t}")
        dma("trq", QFM[:], QTOK[:], ["QTOK"], ["QFM"], transpose=True,
            name=f"trq{t}")
        dma("trk", KFM[:], KTOK[:].rearrange("p a d -> p (a d)"), ["KTOK"], ["KFM"],
            transpose=True, name=f"trk{t}")

    def stC1(t):
        for tc in range(NTC):
            g = t * NTC + tc
            for w in range(4):
                bank = 3 + (tc * 4 + w) % 2
                def em_at(tc=tc, w=w, bank=bank):
                    r = None
                    for j in range(4):
                        h = ATPERM[w * 4 + j]
                        po = (h % 2) * 64
                        r = nc.tensor.matmul(
                            PS[bank][:, j * 128:(j + 1) * 128],
                            KFM[po:po + 64, tc, h // 2, :],
                            QFM[po:po + 64, tc, h // 2, :],
                            start=(j == 0), stop=(j == 3))
                    return r
                op("T", em_at, ["KFM", "QFM"], [f"ps!{bank}"], name=f"at{g}.{w}")
                op("V", (lambda tc=tc, w=w, bank=bank: nc.vector.tensor_tensor(
                    ATS[:, tc, w * 4:(w + 1) * 4, :], PS[bank][:, 0:512], MASK[:],
                    op=ALU.mult)),
                    [f"ps!{bank}", "MASK"], [f"ATS.{tc}"], name=f"ate{g}.{w}")

    def stC2(t):
        for tc in range(NTC):
            g = t * NTC + tc
            groups = [list(range(0, 7)), list(range(7, 14)), [14, 15]]
            for bi, grp in enumerate(groups):
                use_inter = g > 0
                def em_nd(tc=tc, g=g, bi=bi, grp=grp, use_inter=use_inter):
                    r = None
                    for idx, h in enumerate(grp):
                        po = (h % 2) * 64
                        off = idx * (HD + 1)
                        r = nc.tensor.matmul(
                            PS[bi][:, off:off + HD + 1], ATS[:, tc, ATSLOT[h], :],
                            VTOK[:, tc, h, :], start=(idx == 0),
                            stop=(not use_inter) and idx == len(grp) - 1)
                        if use_inter:
                            r = nc.tensor.matmul(
                                PS[bi][:, off:off + HD + 1],
                                QFM[po:po + 64, tc, h // 2, :],
                                STSB[po:po + 64, h // 2, :],
                                start=False, stop=(idx == len(grp) - 1))
                    return r
                rd = [f"ATS.{tc}", "VTOK", "VT1"] + (["QFM", "STSB"] if use_inter else [])
                op("T", em_nd, rd, [f"ps!{bi}"], name=f"nd{g}.{bi}")

            for sb_i in range(2):
                def em_st(tc=tc, g=g, sb_i=sb_i):
                    r = None
                    for j in range(8):
                        h = sb_i * 8 + j
                        po = (h % 2) * 64
                        slot = (h // 2) % 4
                        r = nc.tensor.matmul(
                            SPS[sb_i][po:po + 64, slot * 65:slot * 65 + 65],
                            KTOK[:, tc, h * HD:(h + 1) * HD],
                            VTOK[:, tc, h, :],
                            start=(g == 0 and j <= 1), stop=False,
                            skip_group_check=True,
                            tile_position=(0, 64) if po else None)
                    return r
                op("T", em_st, ["KTOK", "VTOK", "VT1"], [f"ps!s{sb_i}"],
                   name=f"st{g}.{sb_i}")
                op("S", (lambda sb_i=sb_i: nc.scalar.copy(
                    STSB[:, sb_i * 4:(sb_i + 1) * 4, :],
                    SPS[sb_i][:, 0:4 * 65])),
                    [f"ps!s{sb_i}"], ["STSB"], name=f"ste{g}.{sb_i}")

            for bi, grp in enumerate(groups):
                ng = len(grp)
                op("V", (lambda bi=bi, grp=grp, ng=ng: nc.vector.tensor_scalar(
                    DEN[:, grp[0]:grp[0] + ng],
                    PS[bi][:, 0:ng * 65].rearrange("p (h c) -> p h c", c=65)[:, :, 64],
                    float(EPS_DEN), None, op0=ALU.add)),
                    [f"ps!{bi}"], ["DEN"], name=f"den{g}.{bi}")
            op("V", lambda: nc.vector.reciprocal(REC[:], DEN[:]),
               ["DEN"], ["REC"], name=f"rec{g}")
            for bi, grp in enumerate(groups):
                for idx, h in enumerate(grp):
                    eng = "S" if (h % 2) else "V"
                    def em_ne(tc=tc, bi=bi, idx=idx, h=h, eng=eng):
                        if eng == "S":
                            return nc.scalar.activation(
                                ATN[:, tc, h * HD:(h + 1) * HD],
                                PS[bi][:, idx * 65:idx * 65 + HD],
                                AF.Identity, bias=f32(0.0),
                                scale=REC[:, h:h + 1])
                        return nc.vector.tensor_scalar(
                            ATN[:, tc, h * HD:(h + 1) * HD],
                            PS[bi][:, idx * 65:idx * 65 + HD],
                            REC[:, h:h + 1], None, op0=ALU.mult)
                    op(eng, em_ne, [f"ps!{bi}", "REC"], ["ATN"], name=f"ne{g}.{h}")

        dma("trat", AFM[:], ATN[:].rearrange("p a d -> p (a d)"), ["ATN"], ["AFM"],
            transpose=True, name=f"trat{t}")

    def stD(t):
        p = t % 2
        XR = XRES[p]
        xr_res = f"XRES{p}"
        for tc in range(NTC):
            def em_o(tc=tc):
                r = None
                for kc in range(NKC):
                    st = AFM[:, tc, kc, :]
                    for hf in range(2):
                        r = nc.tensor.matmul(PS[3 + hf][:, 0:512], st,
                                             WF_["o"][:, kc, hf * 512:(hf + 1) * 512],
                                             start=(kc == 0), stop=False)
                for hf in range(2):
                    r = nc.tensor.matmul(PS[3 + hf][:, 0:512], ONES1[:],
                                         RMO[:, hf * 512:(hf + 1) * 512],
                                         start=False, stop=True)
                return r
            op("T", em_o, ["AFM", "Wo", "ONES1", "RMO0"],
               ["ps!3", "ps!4"], name=f"oproj{t}.{tc}")
            op("V", (lambda tc=tc: nc.vector.tensor_copy(
                PCOPY[:, tc, 0:512], PS[3][:, 0:512])),
               ["ps!3"], [f"PC.{tc}"], name=f"ocp{t}.{tc}.0")
            op("V", (lambda tc=tc: nc.vector.tensor_copy(
                PCOPY[:, tc, 512:D], PS[4][:, 0:512])),
               ["ps!4"], [f"PC.{tc}"], name=f"ocp{t}.{tc}.1")
            op("V", (lambda tc=tc: nc.vector.bn_stats(BNB[:, 0:6],
                                                      PCOPY[:, tc, 0:512])),
               [f"PC.{tc}"], ["BNB"], name=f"bs2a{t}.{tc}")
            op("V", (lambda tc=tc: nc.vector.bn_stats(BNB[:, 6:12],
                                                      PCOPY[:, tc, 512:D])),
               [f"PC.{tc}"], ["BNB"], name=f"bs2b{t}.{tc}")
            op("V", (lambda tc=tc: nc.vector.bn_aggr(AGG[:, 1, tc, :], BNB[:])),
               ["BNB"], ["AGG2"], name=f"ba2{t}.{tc}")
            op("S", (lambda tc=tc: nc.scalar.activation(ds("tmp", tc), AGG[:, 1, tc, 1:2],
                                                        AF.Ln, bias=EPS_LN)),
               ["AGG2"], ["DS.tmp"], name=f"ln2l{t}.{tc}")
            op("S", (lambda tc=tc: nc.scalar.activation(ds("r2", tc), ds("tmp", tc),
                                                        AF.Exp, bias=f32(0.0),
                                                        scale=f32(-0.5))),
               ["DS.tmp"], ["DS.r2"], name=f"ln2e{t}.{tc}")
            op("V", (lambda tc=tc: nc.vector.scalar_tensor_tensor(
                ds("tsc2", tc), AGG[:, 1, tc, 0:1], -1.0, ds("r2", tc),
                op0=ALU.mult, op1=ALU.mult)),
               ["AGG2", "DS.r2"], ["DS.tsc2"], name=f"tsc2{t}.{tc}")
            op("S", (lambda tc=tc, XR=XR: nc.scalar.activation(
                WBUF[:, tc * D:(tc + 1) * D], XR[:, tc, :], AF.Identity,
                bias=ds("tsc2", tc))),
               [f"{xr_res}.{tc}", "DS.tsc2"], [f"WB.{tc}"], name=f"xmod{t}.{tc}")
            op("V", (lambda tc=tc: nc.vector.scalar_tensor_tensor(
                X2[:, tc, :], PCOPY[:, tc, :], ds("r2", tc),
                WBUF[:, tc * D:(tc + 1) * D],
                op0=ALU.mult, op1=ALU.add)),
                [f"PC.{tc}", "DS.r2", f"WB.{tc}"], ["X2"],
                name=f"x2{t}.{tc}")

    def stE(t):
        p = t % 2
        XR = XRES[p]
        xr_res = f"XRES{p}"
        XRflat = XR[:].rearrange("p a d -> p (a d)")
        for tc in range(NTC):
            op("V", (lambda tc=tc: nc.vector.bn_stats(BNB[:, 0:6], X2[:, tc, 0:512])),
               ["X2"], ["BNB"], name=f"bs3a{t}.{tc}")
            op("V", (lambda tc=tc: nc.vector.bn_stats(BNB[:, 6:12], X2[:, tc, 512:D])),
               ["X2"], ["BNB"], name=f"bs3b{t}.{tc}")
            op("V", (lambda tc=tc: nc.vector.bn_aggr(AGG[:, 2, tc, :], BNB[:])),
               ["BNB"], ["AGG3"], name=f"ba3{t}.{tc}")
        op("S", lambda: nc.scalar.activation(ds("tmp"), AGG[:, 2, :, 1], AF.Ln,
                                             bias=EPS_LN),
           ["AGG3"], ["DS.tmp"], name=f"ln3l{t}")
        op("S", lambda: nc.scalar.activation(ds("r3"), ds("tmp"), AF.Exp,
                                             bias=f32(0.0), scale=f32(-0.5)),
           ["DS.tmp"], ["DS.r3"], name=f"ln3e{t}")
        op("V", lambda: nc.vector.tensor_scalar(ds("nm3"), AGG[:, 2, :, 0], -1.0,
                                                None, op0=ALU.mult),
           ["AGG3"], ["DS.nm3"], name=f"nm3{t}")
        for tc in range(NTC):
            op("S", (lambda tc=tc: nc.scalar.activation(
                EQ[:, tc * D:(tc + 1) * D], X2[:, tc, :], AF.Abs,
                bias=ds("nm3", tc))),
                ["X2", "DS.nm3"], ["EQ"], name=f"absd{t}.{tc}")
            op("V", (lambda tc=tc: nc.vector.tensor_reduce(
                ds("amx", tc), EQ[:, tc * D:(tc + 1) * D], axis=AX.X, op=ALU.max)),
               ["EQ"], ["DS.amx"], name=f"amx{t}.{tc}")
        op("V", lambda: nc.vector.tensor_tensor(ds("tmp2"), ds("amx"), ds("r3"),
                                                op=ALU.mult),
           ["DS.amx", "DS.r3"], ["DS.tmp2"], name=f"aml{t}")
        op("V", lambda: nc.vector.tensor_scalar(ds("xs"), ds("tmp2"), float(EPS_Q),
                                                1.0 / QMAX, op0=ALU.max,
                                                op1=ALU.mult),
           ["DS.tmp2"], ["DS.xs"], name=f"xs{t}")
        op("V", lambda: nc.vector.reciprocal(ds("sinv"), ds("xs")),
           ["DS.xs"], ["DS.sinv"], name=f"sinv{t}")
        op("V", lambda: nc.vector.tensor_scalar(ds("nsinv"), ds("sinv"), -1.0,
                                                None, op0=ALU.mult),
           ["DS.sinv"], ["DS.nsinv"], name=f"nsinv{t}")
        op("V", lambda: nc.vector.tensor_tensor(ds("rs"), ds("r3"), ds("sinv"),
                                                op=ALU.mult),
           ["DS.r3", "DS.sinv"], ["DS.rs"], name=f"rs{t}")
        op("V", lambda: nc.vector.scalar_tensor_tensor(ds("tmp2"), ds("nm3"),
                                                       1.0, ds("rs"),
                                                       op0=ALU.mult, op1=ALU.mult),
           ["DS.nm3", "DS.rs"], ["DS.tmp2"], name=f"s2a{t}")
        op("V", lambda: nc.vector.tensor_scalar(ds("s2"), ds("tmp2"), 0.5,
                                                None, op0=ALU.add),
           ["DS.tmp2"], ["DS.s2"], name=f"s2{t}")
        op("V", lambda: nc.vector.tensor_scalar(ds("nrs"), ds("rs"), -1.0,
                                                None, op0=ALU.mult),
           ["DS.rs"], ["DS.nrs"], name=f"nrs{t}")
        op("V", lambda: nc.vector.tensor_scalar(ds("s2n"), ds("s2"), 127.5, -1.0,
                                                op0=ALU.add, op1=ALU.mult),
           ["DS.s2"], ["DS.s2n"], name=f"s2n{t}")

    def stEb(t):
        p = t % 2
        XR = XRES[p]
        xr_res = f"XRES{p}"
        XRflat = XR[:].rearrange("p a d -> p (a d)")
        for tc in range(NTC):
            op("S", (lambda tc=tc: nc.scalar.activation(
                WBUF[:, tc * D:(tc + 1) * D], X2[:, tc, :], AF.Identity,
                bias=ds("s2", tc), scale=ds("rs", tc))),
                ["X2", "DS.s2", "DS.rs"], [f"WB.{tc}"], name=f"pw{t}.{tc}")
        for k in range(8):
            half = float(2 ** (6 - k))
            for tc in range(NTC):
                dsl = slice(tc * D, (tc + 1) * D)
                vin = WBUF[:, dsl] if k % 2 == 0 else XRflat[:, dsl]
                vout = XRflat[:, dsl] if k % 2 == 0 else WBUF[:, dsl]
                vin_r = f"WB.{tc}" if k % 2 == 0 else f"{xr_res}.{tc}"
                vout_r = f"{xr_res}.{tc}" if k % 2 == 0 else f"WB.{tc}"
                op("S", (lambda vin=vin, dsl=dsl: nc.scalar.activation(
                    SGB[:, dsl], vin, AF.Sign)),
                   [vin_r], [f"SG.{tc}"], name=f"qsg{t}.{k}.{tc}")
                op("V", (lambda vin=vin, vout=vout, half=half, dsl=dsl:
                         nc.vector.scalar_tensor_tensor(vout, SGB[:, dsl], -half,
                                                        vin, op0=ALU.mult,
                                                        op1=ALU.add)),
                   [f"SG.{tc}", vin_r], [vout_r], name=f"qvs{t}.{k}.{tc}")
        for tc in range(NTC):
            op("V", (lambda tc=tc, XRflat=XRflat: nc.vector.scalar_tensor_tensor(
                XRflat[:, tc * D:(tc + 1) * D], X2[:, tc, :], ds("nrs", tc),
                WBUF[:, tc * D:(tc + 1) * D], op0=ALU.mult, op1=ALU.add)),
               ["X2", "DS.nrs", f"WB.{tc}"], [f"{xr_res}.{tc}"], name=f"xqa{t}.{tc}")
            op("S", (lambda tc=tc, XRflat=XRflat: nc.scalar.activation(
                XQT[:, tc * D:(tc + 1) * D], XRflat[:, tc * D:(tc + 1) * D],
                AF.Identity, bias=ds("s2n", tc))),
               [f"{xr_res}.{tc}", "DS.s2n"], ["EQ"], name=f"xqb{t}.{tc}")
        dma("trxq", XQFM[:], XQT[:], ["EQ"], ["XQFM"], transpose=True,
            name=f"trxq{t}")

    def stE2(t):
        for tc in range(NTC):
            op("T", (lambda tc=tc: nc.tensor.transpose(
                PS[5][0:1, 384:512], ds("nsinv", tc), IDENT[:])),
                ["DS.nsinv", "IDENT"], ["ps!5"], name=f"t3{t}.{tc}")
            op("S", (lambda tc=tc: nc.scalar.copy(
                ROWF[0:1, tc, :], PS[5][0:1, 384:512])),
                ["ps!5"], ["ROWF0"], name=f"t3e{t}.{tc}")

    def stF(t):
        p = t % 2
        XR = XRES[p]
        xr_res = f"XRES{p}"
        for tc in range(NTC):
            def em_f(tc=tc):
                r = None
                for kc in range(NKC):
                    st = XQFM[:, tc, kc, :]
                    for hf in range(2):
                        r = nc.tensor.matmul(PS[3 + hf][:, 0:512], st,
                                             WDQF[:, kc, hf * 512:(hf + 1) * 512],
                                             start=(kc == 0), stop=False)
                for hf in range(2):
                    r = nc.tensor.matmul(PS[3 + hf][:, 0:512], ONES1[:],
                                         RMFA[:, hf * 512:(hf + 1) * 512],
                                         start=False, stop=False)
                    r = nc.tensor.matmul(PS[3 + hf][:, 0:512], ROWF[0:1, tc, :],
                                         RM["f"][0:1, hf * 512:(hf + 1) * 512],
                                         start=False, stop=True)
                return r
            op("T", em_f, ["XQFM", "WDQF", "ROWF0", "RMf0", "RMFA", "ONES1"],
               ["ps!3", "ps!4"], name=f"ffn{t}.{tc}")
            op("V", (lambda tc=tc: nc.vector.tensor_copy(
                PCOPY[:, tc, 0:512], PS[3][:, 0:512])),
               ["ps!3"], [f"PC.{tc}"], name=f"fcp{t}.{tc}.0")
            op("V", (lambda tc=tc: nc.vector.tensor_copy(
                PCOPY[:, tc, 512:D], PS[4][:, 0:512])),
               ["ps!4"], [f"PC.{tc}"], name=f"fcp{t}.{tc}.1")
            op("V", (lambda tc=tc: nc.vector.bn_stats(BNB[:, 0:6],
                                                      PCOPY[:, tc, 0:512])),
               [f"PC.{tc}"], ["BNB"], name=f"bs4a{t}.{tc}")
            op("V", (lambda tc=tc: nc.vector.bn_stats(BNB[:, 6:12],
                                                      PCOPY[:, tc, 512:D])),
               [f"PC.{tc}"], ["BNB"], name=f"bs4b{t}.{tc}")
            op("V", (lambda tc=tc: nc.vector.bn_aggr(AGG[:, 3, tc, :], BNB[:])),
               ["BNB"], ["AGG4"], name=f"ba4{t}.{tc}")
            op("V", (lambda tc=tc: nc.vector.tensor_tensor(
                ds("xs2", tc), ds("xs", tc), ds("xs", tc), op=ALU.mult)),
               ["DS.xs"], ["DS.xs2"], name=f"xs2{t}.{tc}")
            op("V", (lambda tc=tc: nc.vector.tensor_tensor(
                ds("vf", tc), AGG[:, 3, tc, 1:2], ds("xs2", tc), op=ALU.mult)),
               ["AGG4", "DS.xs2"], ["DS.vf"], name=f"vf{t}.{tc}")
            op("S", (lambda tc=tc: nc.scalar.activation(ds("tmp", tc), ds("vf", tc),
                                                        AF.Ln, bias=EPS_LN)),
               ["DS.vf"], ["DS.tmp"], name=f"ln4l{t}.{tc}")
            op("S", (lambda tc=tc: nc.scalar.activation(ds("r4", tc), ds("tmp", tc),
                                                        AF.Exp, bias=f32(0.0),
                                                        scale=f32(-0.5))),
               ["DS.tmp"], ["DS.r4"], name=f"ln4e{t}.{tc}")
            op("V", (lambda tc=tc: nc.vector.scalar_tensor_tensor(
                ds("sfin", tc), ds("xs", tc), -1.0, ds("r4", tc),
                op0=ALU.mult, op1=ALU.mult)),
               ["DS.xs", "DS.r4"], ["DS.sfin"], name=f"sfin{t}.{tc}")
            op("V", (lambda tc=tc: nc.vector.tensor_scalar(
                ds("nmp", tc), AGG[:, 3, tc, 0:1], -1.0, None, op0=ALU.mult)),
               ["AGG4"], ["DS.nmp"], name=f"nmp{t}.{tc}")
            op("V", (lambda tc=tc: nc.vector.tensor_tensor(
                ds("tscf", tc), ds("nmp", tc), ds("sfin", tc), op=ALU.mult)),
               ["DS.nmp", "DS.sfin"], ["DS.tscf"], name=f"tscf{t}.{tc}")
            op("S", (lambda tc=tc: nc.scalar.activation(
                WBUF[:, tc * D:(tc + 1) * D], X2[:, tc, :], AF.Identity,
                bias=ds("tscf", tc))),
               ["X2", "DS.tscf"], [f"WB.{tc}"], name=f"fmod{t}.{tc}")
            op("V", (lambda tc=tc, XR=XR: nc.vector.scalar_tensor_tensor(
                XR[:, tc, :], PCOPY[:, tc, :], ds("sfin", tc),
                WBUF[:, tc * D:(tc + 1) * D],
                op0=ALU.mult, op1=ALU.add)),
                [f"PC.{tc}", "DS.sfin", f"WB.{tc}"], [f"{xr_res}.{tc}"],
                name=f"fin{t}.{tc}")
        dst = out_h[t * TT:(t + 1) * TT, :].rearrange("(a b) d -> b a d", a=NTC)
        dma(f"out{p}", dst, XR[:], [f"{xr_res}.0", f"{xr_res}.1"], [f"OUT{t}"],
            name=f"store{t}")

    stA(0)
    stA2(0)
    stB(0)
    st_wquant()
    stC1(0)
    stC2(0)
    for t in range(NTILE):
        stD(t)
        if t + 1 < NTILE:
            stA(t + 1)
            stA2(t + 1)
            stB(t + 1)
        stE(t)
        stE2(t)
        if t + 1 < NTILE:
            stC1(t + 1)
        stEb(t)
        if t + 1 < NTILE:
            stC2(t + 1)
        stF(t)

    sems = {}
    for name in sc.counts:
        sems[name] = es.enter_context(nc.semaphore(name=f"sem_{name}".replace(".", "_")))

    by_engine = {}
    for o in sc.ops:
        by_engine.setdefault(o["engine"], []).append(o)

    def run_ops(proxy, key):
        for o in by_engine.get(key, ()):
            for s, v in o["waits"]:
                proxy.wait_ge(sems[s], v)
            r = o["emit"]()
            if isinstance(r, (list, tuple)):
                r = r[-1]
            assert r is not None, f"op {o['name']} returned no instruction"
            r.then_inc(sems[o["sem"]], o["step"])

    with nc.Block() as blk:
        @blk.sync
        def _(q):
            run_ops(q, "Q")
            for s in sc.dma_sems:
                q.wait_ge(sems[s], sc.counts[s])

        @blk.gpsimd
        def _(gp):
            run_ops(gp, "G")

        @blk.tensor
        def _(tt_):
            run_ops(tt_, "T")

        @blk.vector
        def _(v):
            run_ops(v, "V")

        @blk.scalar
        def _(s_):
            run_ops(s_, "S")

    es.close()
    return nc


_NC_CACHE = {}


def _consts():
    mask = np.triu(np.ones((128, 128), dtype=np.float32))
    maskrep = np.tile(mask, (1, 4)).astype(ml_dtypes.bfloat16)
    ident = np.eye(128, dtype=np.float32)
    return maskrep, ident


def kernel(L0_attn_x, L0_attn_mask, L0_attn_Wq, L0_attn_bq, L0_attn_Wk,
           L0_attn_bk, L0_attn_Wv, L0_attn_bv, L0_attn_Wo, L0_attn_bo,
           L0_ffn_weight, L0_ffn_bias):
    x = np.ascontiguousarray(np.asarray(L0_attn_x, dtype=np.float32))
    B, S, D_ = x.shape
    assert D_ == D
    maskrep, ident = _consts()
    shared = {
        "wq": np.ascontiguousarray(np.asarray(L0_attn_Wq, dtype=np.float32)),
        "wk": np.ascontiguousarray(np.asarray(L0_attn_Wk, dtype=np.float32)),
        "wv": np.ascontiguousarray(np.asarray(L0_attn_Wv, dtype=np.float32)),
        "wo": np.ascontiguousarray(np.asarray(L0_attn_Wo, dtype=np.float32)),
        "wf": np.ascontiguousarray(np.asarray(L0_ffn_weight, dtype=np.float32)),
        "bq": np.ascontiguousarray(np.asarray(L0_attn_bq, dtype=np.float32)),
        "bk": np.ascontiguousarray(np.asarray(L0_attn_bk, dtype=np.float32)),
        "bv": np.ascontiguousarray(np.asarray(L0_attn_bv, dtype=np.float32)),
        "bo": np.ascontiguousarray(np.asarray(L0_attn_bo, dtype=np.float32)),
        "bf": np.ascontiguousarray(np.asarray(L0_ffn_bias, dtype=np.float32)),
        "maskc": maskrep,
        "identc": ident,
    }
    key = S
    if key not in _NC_CACHE:
        _NC_CACHE[key] = build_program(S)
    nc = _NC_CACHE[key]
    in_maps = [dict(shared, x=np.ascontiguousarray(x[b])) for b in range(B)]
    res = run_bass_kernel_spmd(nc, in_maps, core_ids=list(range(B)))
    out = np.stack([np.asarray(r["out"]) for r in res.results], axis=0)
    return out.astype(np.float32)
